# version 16
# baseline (speedup 1.0000x reference)
import contextlib
import numpy as np
import concourse.bass as bass
import concourse.mybir as mybir
from concourse.bass_utils import run_bass_kernel_spmd

dt = mybir.dt
F32 = dt.float32
BF16 = dt.bfloat16
FP8 = dt.float8e4
AF = mybir.ActivationFunctionType
ALU = mybir.AluOpType

D = 1024
S = 2048
H = 16
P = 64
NC = 8
DC = D // 128
SC = S // 128
NPAIR = H // 2
EXP_SHIFT = -16.0


class Emitter:
    def __init__(self):
        self.ops = {e: [] for e in ("sync", "gpsimd", "vector", "scalar", "tensor")}
        self.counts = {}
        self.last_wait = {}

    def inc(self, sem, n=1):
        self.counts[sem] = self.counts.get(sem, 0) + n
        return self.counts[sem]

    def wait(self, engine, sem, val):
        if val is None or (isinstance(val, int) and val <= 0):
            return
        key = (engine, sem)
        if self.last_wait.get(key, -1) >= val:
            return
        self.last_wait[key] = val
        self.ops[engine].append(("wait", sem, val))

    def op(self, engine, fn, inc_sem=None, inc_n=1):
        val = self.inc(inc_sem, inc_n) if inc_sem else None
        self.ops[engine].append(("op", fn, inc_sem, inc_n))
        return val


def build_program(dbg=False, paranoid=False):
    nc = bass.Bass("TRN2", target_bir_lowering=False, debug=False)

    x_h = nc.dram_tensor("x", [D, S], F32, kind="ExternalInput").ap()
    wq_h = nc.dram_tensor("wq", [D, D], F32, kind="ExternalInput").ap()
    wk_h = nc.dram_tensor("wk", [D, D], F32, kind="ExternalInput").ap()
    wv_h = nc.dram_tensor("wv", [D, D], F32, kind="ExternalInput").ap()
    wl_h = nc.dram_tensor("wl", [D, D], F32, kind="ExternalInput").ap()
    fk_h = nc.dram_tensor("fk", [D, D], BF16, kind="ExternalInput").ap()
    qb_h = nc.dram_tensor("qb", [128, NPAIR], F32, kind="ExternalInput").ap()
    kb_h = nc.dram_tensor("kb", [128, NPAIR], F32, kind="ExternalInput").ap()
    vbr_h = nc.dram_tensor("vbr", [128, D], F32, kind="ExternalInput").ap()
    lbr_h = nc.dram_tensor("lbr", [1, D], F32, kind="ExternalInput").ap()
    fbr_h = nc.dram_tensor("fbr", [1, D], BF16, kind="ExternalInput").ap()
    sc_h = nc.dram_tensor("sclr", [128, 256], F32, kind="ExternalInput").ap()
    y_h = nc.dram_tensor("y", [D, S], F32, kind="ExternalOutput").ap()

    em = Emitter()
    W, OP = em.wait, em.op
    cm = contextlib.ExitStack()
    with cm:
        E = cm.enter_context
        xbf = [E(nc.sbuf_tensor(f"xbf{i}", [128, S], BF16)) for i in range(2)]
        xn = E(nc.sbuf_tensor("xn", [128, DC * S], BF16))
        vsb = E(nc.sbuf_tensor("vsb", [128, SC * D], BF16))
        ctx_sb = E(nc.sbuf_tensor("ctxs", [128, NPAIR * S], BF16))
        wq_sb = E(nc.sbuf_tensor("wqs", [128, DC * D], BF16))
        wk_sb = E(nc.sbuf_tensor("wks", [128, DC * D], BF16))
        uE = E(nc.sbuf_tensor("uE", [128, 4 * S], BF16))
        uE2 = E(nc.sbuf_tensor("uE2", [128, 2 * S], BF16))
        qt = [E(nc.sbuf_tensor(f"qt{i}", [128, S], BF16)) for i in range(2)]
        xsq = [qt[0], qt[0]]
        kt = [E(nc.sbuf_tensor(f"kt{i}", [128, S], BF16)) for i in range(2)]
        x32 = [E(nc.sbuf_tensor(f"x32{i}", [128, S], F32)) for i in range(2)]
        f8 = E(nc.sbuf_tensor("f8", [128, 1024], FP8))
        m_row = x32[1][0:1, :]
        v_row = x32[0][0:1, :]
        vbr = E(nc.sbuf_tensor("vbrs", [128, D], BF16))
        qb_sb = E(nc.sbuf_tensor("qbs", [128, NPAIR], F32))
        kb_sb = E(nc.sbuf_tensor("kbs", [128, NPAIR], F32))
        lbr_sb = E(nc.sbuf_tensor("lbrs", [1, D], F32))
        fbr_sb = E(nc.sbuf_tensor("fbrs", [1, D], BF16))
        bcomb_row = E(nc.sbuf_tensor("bcombr", [1, D], BF16))
        sc_sb = E(nc.sbuf_tensor("scs", [128, 256], F32))
        srp = E(nc.sbuf_tensor("srp", [128, 3 * 128], F32))
        ones512b = E(nc.sbuf_tensor("ones512b", [1, 512], BF16))
        ones_row = E(nc.sbuf_tensor("onesr", [1, 128], F32))
        c1024 = E(nc.sbuf_tensor("c1024", [128, 1], BF16))
        expb = E(nc.sbuf_tensor("expb", [128, 1], F32))
        exps = E(nc.sbuf_tensor("exps", [128, 1], F32))
        eps_col = E(nc.sbuf_tensor("epsc", [1, 1], F32))
        one_col = E(nc.sbuf_tensor("onec", [1, 1], F32))
        acc = [E(nc.sbuf_tensor(f"acc{i}", [128, 4], F32)) for i in range(2)]
        flushb = E(nc.sbuf_tensor("flushb", [128, 32], BF16))
        vp = [E(nc.sbuf_tensor(f"vp{i}", [128, 128], BF16)) for i in range(2)]

        PSA = E(nc.psum_tensor("psa", [128, S], F32))
        PSB = E(nc.psum_tensor("psb", [128, S], F32))

        sems = {n: E(nc.semaphore(n)) for n in
                ["xa", "xb", "xc", "xd", "w", "wb", "wc", "w2", "sm",
                 "x3a", "x3b", "sta", "stb", "pe", "ve", "ac"]}
        block = E(nc.Block())

        OP("vector", lambda e, s: e.memset(ones_row[:], 1.0))
        OP("vector", lambda e, s: e.memset(c1024[:], 1.0 / 1024.0))
        OP("vector", lambda e, s: e.memset(expb[:], EXP_SHIFT))
        OP("vector", lambda e, s: e.memset(eps_col[:], 1e-5))
        OP("vector", lambda e, s: e.memset(one_col[:], 1.0))
        OP("vector", lambda e, s: e.memset(ones512b[:], 1.0))
        OP("vector", lambda e, s: e.memset(x32[0][32:33, 0:128], 1.0))
        ve_consts = OP("vector", lambda e, s: e.memset(exps[:], 0.125), "ve")

        sm_small = None
        for (dst, src) in ((qb_sb, qb_h), (kb_sb, kb_h),
                           (lbr_sb, lbr_h), (fbr_sb, fbr_h), (sc_sb, sc_h)):
            sm_small = OP("sync",
                          (lambda d, s0: lambda e, s: e.dma_start(d[:], s0))(dst, src),
                          "sm", 16)

        def wload(dst_t, src_ap):
            def f(e, s):
                o = dst_t[:].rearrange("p (c h) -> p c h", c=DC)
                i = src_ap.rearrange("(c p) h -> p c h", p=128)
                return e.dma_start(o, i)
            return f

        W("vector", sems["sm"], sm_small)
        ve_bc = OP("vector", lambda e, s: e.tensor_add(bcomb_row[:], lbr_sb[:], fbr_sb[:]), "ve")
        OP("vector", lambda e, s: e.reciprocal(srp[:, 0:128], sc_sb[:, 0:128]))
        OP("vector", lambda e, s: e.reciprocal(srp[:, 128:256], sc_sb[:, 128:256]))
        ve_srep = OP("vector", lambda e, s: e.tensor_mul(
            srp[:, 256:384], sc_sb[:, 0:128], sc_sb[:, 128:256]), "ve")

        ve_xsq = [None] * DC
        pe_stats = [None] * DC
        for c in range(DC):
            par = ("xa", "xb", "xc", "xd")[c % 4]
            if c >= 4:
                W("gpsimd", sems["ve"], ve_xsq[c - 4])
                W("gpsimd", sems["pe"], pe_stats[c - 4])
            xd = OP("gpsimd",
                    (lambda c: lambda e, s: e.dma_start(
                        xn[:, c * S:(c + 1) * S],
                        x_h[c * 128:(c + 1) * 128, :]))(c), par, 16)
            W("vector", sems[par], xd)
            if c >= 1:
                W("vector", sems["pe"], pe_stats[c - 1])
            ve_xsq[c] = OP("vector",
                           (lambda c: lambda e, s: e.tensor_mul(
                               xsq[c % 2][:], xn[:, c * S:(c + 1) * S],
                               xn[:, c * S:(c + 1) * S]))(c), "ve")
            W("tensor", sems[par], xd)
            W("tensor", sems["ve"], ve_xsq[c])
            if c == 0:
                W("tensor", sems["ve"], ve_consts)
                W("tensor", sems["ve"], ve_srep)
            def stats(c):
                def f(e, s):
                    last = None
                    for sl in range(4):
                        last = e.matmul(PSA[0:1, sl * 512:(sl + 1) * 512], c1024[:],
                                        xn[:, c * S + sl * 512: c * S + (sl + 1) * 512],
                                        start=(c == 0), stop=(c == DC - 1),
                                        skip_group_check=True)
                        last = e.matmul(PSB[0:1, sl * 512:(sl + 1) * 512], c1024[:],
                                        xsq[c % 2][:, sl * 512:(sl + 1) * 512],
                                        start=(c == 0), stop=(c == DC - 1),
                                        skip_group_check=True)
                    return last
                return f
            pe_stats[c] = OP("tensor", stats(c), "pe")

        OP("gpsimd", wload(uE, wv_h), "w", 16)
        OP("gpsimd", lambda e, s: e.dma_start(vbr[:], vbr_h), "wb", 16)
        OP("gpsimd", wload(wq_sb, wq_h), "wc", 16)
        w_done = OP("gpsimd", wload(wk_sb, wk_h), "wc", 16)

        W("vector", sems["pe"], pe_stats[DC - 1])
        OP("vector", lambda e, s: e.tensor_copy(m_row, PSA[0:1, :]))
        OP("vector", lambda e, s: e.tensor_mul(v_row, m_row, m_row))
        ve_var = OP("vector", lambda e, s: e.tensor_sub(v_row, PSB[0:1, :], v_row), "ve")

        def repl(ps, row, ones):
            def f(e, s):
                last = None
                for sl in range(4):
                    last = e.matmul(ps[:, sl * 512:(sl + 1) * 512], ones,
                                    row[:, sl * 512:(sl + 1) * 512], start=True, stop=True)
                return last
            return f
        std_row = x32[1][64:65, :]
        rstd_row = x32[1][32:33, :]
        ones32 = x32[0][32:33, 0:128]
        W("tensor", sems["ve"], ve_var)
        pe_mrep = OP("tensor", repl(PSA, m_row, ones_row[:]), "pe")
        W("scalar", sems["ve"], ve_var)
        ac_sqrt = OP("scalar", lambda e, s: e.activation(
            std_row, v_row, AF.Sqrt, bias=eps_col[:], scale=one_col[:]), "ac")
        W("vector", sems["ac"], ac_sqrt)
        ve_rstd = OP("vector", lambda e, s: e.reciprocal(rstd_row, std_row), "ve")
        W("tensor", sems["ve"], ve_rstd)
        pe_rrep = OP("tensor", repl(PSB, rstd_row, ones32), "pe")

        ve_xn_h = [None, None]
        W("vector", sems["pe"], pe_rrep)
        for hh in range(2):
            t0 = hh * 1024
            for c in range(DC):
                def lnapply(c, t0):
                    def f(e, s):
                        e.tensor_sub(xsq[0][:, t0:t0 + 1024],
                                     xn[:, c * S + t0: c * S + t0 + 1024],
                                     PSA[:, t0:t0 + 1024])
                        return e.tensor_mul(xn[:, c * S + t0: c * S + t0 + 1024],
                                            xsq[0][:, t0:t0 + 1024],
                                            PSB[:, t0:t0 + 1024])
                    return f
                ve_xn_h[hh] = OP("vector", lnapply(c, t0), "ve")
        ve_xn = [ve_xn_h[1]] * DC

        W("tensor", sems["ve"], ve_xn_h[0])
        W("tensor", sems["w"], 16)
        ve_vev = [None] * SC
        pe_v = [None] * SC
        W("vector", sems["wb"], 16)
        for j in range(SC):
            half = (j % 2) * 1024
            if j == 1:
                W("tensor", sems["ve"], ve_xn_h[1])
            if j >= 2:
                W("tensor", sems["ve"], ve_vev[j - 2])
            def vproj(j, half):
                def f(e, s):
                    last = None
                    for c in range(DC):
                        for sl in range(2):
                            last = e.matmul(
                                PSA[:, half + sl * 512: half + (sl + 1) * 512],
                                xn[:, c * S + j * 128: c * S + (j + 1) * 128],
                                uE[:, c * D + sl * 512: c * D + (sl + 1) * 512],
                                start=(c == 0), stop=(c == DC - 1),
                                skip_group_check=True)
                    return last
                return f
            pe_v[j] = OP("tensor", vproj(j, half), "pe")
            W("vector", sems["pe"], pe_v[j])
            ve_vev[j] = OP("vector",
                           (lambda j, half: lambda e, s: e.tensor_add(
                               vsb[:, j * D:(j + 1) * D], PSA[:, half:half + 1024],
                               vbr[:]))(j, half), "ve")

        def PARA(engine):
            if not paranoid:
                return
            for snm in ("pe", "ve", "ac"):
                v = em.counts.get(snm, 0)
                if v > 0:
                    W(engine, sems[snm], v)
        pe_qk = [None] * NPAIR
        ve_qk_ready = [None] * NPAIR
        quant_q = []
        quant_done = {}

        def kq_op(sl):
            def f(e, s):
                last = None
                for hf in range(2):
                    for st_ in range(8):
                        a = sl * S + hf * 1024 + st_ * 128
                        e.tensor_mul(f8[:, st_ * 128:(st_ + 1) * 128],
                                     wk_sb[:, a:a + 128], srp[:, 128:256])
                    for st_ in range(8):
                        a = sl * S + hf * 1024 + st_ * 128
                        last = e.tensor_mul(wk_sb[:, a:a + 128],
                                            f8[:, st_ * 128:(st_ + 1) * 128],
                                            srp[:, 256:384])
                return last
            return f

        def xq_op(c):
            def f(e, s):
                last = None
                for hf in range(2):
                    for st_ in range(8):
                        a = c * S + hf * 1024 + st_ * 128
                        e.tensor_mul(f8[:, st_ * 128:(st_ + 1) * 128],
                                     xn[:, a:a + 128], srp[:, 0:128])
                    last = e.tensor_copy(xn[:, c * S + hf * 1024: c * S + hf * 1024 + 1024],
                                         f8[:])
                return last
            return f

        for _c in range(DC):
            quant_q.append(("xq", xq_op(_c)))
        for _sl in range(4):
            quant_q.append(("kq", kq_op(_sl)))
        ve_ctxev = [None] * NPAIR
        pe_s = {}
        ac_e = {}
        ve_vp = {}
        pe_ctx = {}
        pending_ctx = [None]
        emit_ctx_holder = [None]

        def emit_final_ctx(em_ctx_fn):
            pp = pending_ctx[0]
            if pp is None:
                return
            pending_ctx[0] = None
            em_ctx_fn(pp, SC - 1)
            W("vector", sems["pe"], pe_ctx[(pp, SC - 1)])
            ve_ctxev[pp] = OP("vector",
                              (lambda pp: lambda e, s: e.tensor_copy(
                                  ctx_sb[:, pp * S:(pp + 1) * S], PSB[:, :]))(pp), "ve")

        for p in range(NPAIR):
            if p == 0:
                W("tensor", sems["ve"], ve_vev[SC - 1])
                W("tensor", sems["wc"], 16 * 2)
            else:
                W("tensor", sems["ac"], ac_e[(p - 1, SC - 1, 1, 1)])

            def qk_slice(p, wsb, ps, pslo, sl):
                def f(e, s):
                    last = None
                    for c in range(DC):
                        last = e.matmul(
                            ps[:, pslo + sl * 512: pslo + (sl + 1) * 512],
                            wsb[:, c * D + p * 128: c * D + p * 128 + 128],
                            xn[:, c * S + sl * 512: c * S + (sl + 1) * 512],
                            start=(c == 0), stop=(c == DC - 1),
                            skip_group_check=True)
                    return last
                return f

            def emit_qk(p, wsb, ps, dst, bias):
                last_ve = None
                for sl in range(4):
                    pe_sl = OP("tensor", qk_slice(p, wsb, ps, 0, sl), "pe")
                    W("vector", sems["pe"], pe_sl)
                    def ev(dst, ps, bias, sl, p):
                        def f(e, s):
                            return e.tensor_scalar(
                                dst[p % 2][:, sl * 512:(sl + 1) * 512],
                                ps[:, sl * 512:(sl + 1) * 512],
                                bias[:, p:p + 1], None, ALU.add)
                        return f
                    last_ve = OP("vector", ev(dst, ps, bias, sl, p), "ve")
                return last_ve

            emit_qk(p, wq_sb, PSA, qt, qb_sb)
            if pending_ctx[0] is not None:
                emit_final_ctx(emit_ctx_holder[0])
                W("tensor", sems["ve"], ve_ctxev[p - 1])
            ve_qk_ready[p] = emit_qk(p, wk_sb, PSB, kt, kb_sb)
            pe_qk[p] = em.counts["pe"]

            def eslc(kc, head, lo, ln):
                par = kc % 4
                if par < 2:
                    return uE[:, par * 2 * S + head * S + lo: par * 2 * S + head * S + lo + ln]
                if par == 2:
                    return uE2[:, head * S + lo: head * S + lo + ln]
                return xbf[head][:, lo: lo + ln]

            def emit_exps(p, kc, h):
                W("scalar", sems["ve"], ve_consts)
                if h == 0:
                    if (p, kc - 2) in ve_vp:
                        W("scalar", sems["ve"], ve_vp[(p, kc - 2)])
                    elif p >= 1 and kc < 2:
                        W("scalar", sems["ve"], ve_vp[(p - 1, SC - 2 + kc)])
                    if (p, kc - 4) in pe_ctx:
                        W("scalar", sems["pe"], pe_ctx[(p, kc - 4)])
                    elif p >= 1 and kc < 4:
                        W("scalar", sems["pe"], pe_ctx[(p - 1, SC - 4 + kc)])
                for head in range(2):
                    W("scalar", sems["pe"], pe_s[(p, kc, h, head)])
                    apar = kc % 2
                    def egrp(kc, head, h, apar):
                        def f(e, s):
                            return e.activation(
                                eslc(kc, head, h * 1024, 1024),
                                PSA[:, head * 1024: head * 1024 + 1024],
                                AF.Exp, bias=expb[:], scale=exps[:],
                                accum_out=acc[apar][:, h * 2 + head: h * 2 + head + 1])
                        return f
                    PARA("scalar")
                    ac_e[(p, kc, h, head)] = OP("scalar", egrp(kc, head, h, apar), "ac")

            def emit_zvp(p, j, ac_gate):
                W("vector", sems["ac"], ac_gate)
                if (p, j - 2) in pe_ctx:
                    W("vector", sems["pe"], pe_ctx[(p, j - 2)])
                elif p >= 1 and j < 2:
                    W("vector", sems["pe"], pe_ctx[(p - 1, SC - 2 + j)])
                def zvp(p, j):
                    par = j % 2
                    def f(e, s):
                        e.tensor_add(sc_sb[:, 0:64],
                                     acc[par][:, 0:1].broadcast_to([128, 64]),
                                     acc[par][:, 2:3].broadcast_to([128, 64]))
                        e.tensor_add(sc_sb[:, 64:128],
                                     acc[par][:, 1:2].broadcast_to([128, 64]),
                                     acc[par][:, 3:4].broadcast_to([128, 64]))
                        e.reciprocal(sc_sb[:, 128:256], sc_sb[:, 0:128])
                        return e.tensor_mul(
                            vp[par][:], vsb[:, j * D + p * 128: j * D + (p + 1) * 128],
                            sc_sb[:, 128:256])
                    return f
                PARA("vector")
                ve_vp[(p, j)] = OP("vector", zvp(p, j), "ve")

            def emit_ctx(p, kc1):
                W("tensor", sems["ve"], ve_vp[(p, kc1)])
                def cgrp(p, kc1):
                    vpar = kc1 % 2
                    def f(e, s):
                        last = None
                        for sl in range(4):
                            last = e.matmul(
                                PSB[0:64, sl * 512:(sl + 1) * 512],
                                vp[vpar][:, 0:64],
                                eslc(kc1, 0, sl * 512, 512),
                                start=(kc1 == 0), stop=(kc1 == SC - 1),
                                tile_position=(0, 0), skip_group_check=True)
                            last = e.matmul(
                                PSB[64:128, sl * 512:(sl + 1) * 512],
                                vp[vpar][:, 64:128],
                                eslc(kc1, 1, sl * 512, 512),
                                start=(kc1 == 0), stop=(kc1 == SC - 1),
                                tile_position=(0, 64), skip_group_check=True)
                        return last
                    return f
                PARA("tensor")
                pe_ctx[(p, kc1)] = OP("tensor", cgrp(p, kc1), "pe")
            emit_ctx_holder[0] = emit_ctx

            def emit_sgrp(p, kc, h, head):
                if kc == 0 and h == 0:
                    if head == 0:
                        W("tensor", sems["ve"], ve_qk_ready[p])
                else:
                    prev = (p, kc, 0, head) if h == 1 else (p, kc - 1, 1, head)
                    W("tensor", sems["ac"], ac_e[prev])
                r0, r1 = (0, 64) if head == 0 else (64, 128)
                pbase = head * 1024
                def sgrp(p, kc, h, head, r0, r1, pbase):
                    def f(e, s):
                        last = None
                        for sl in range(2):
                            q0 = h * 1024 + sl * 512
                            last = e.matmul(
                                PSA[:, pbase + sl * 512: pbase + (sl + 1) * 512],
                                kt[p % 2][r0:r1, kc * 128:(kc + 1) * 128],
                                qt[p % 2][r0:r1, q0:q0 + 512],
                                start=True, stop=True)
                        return last
                    return f
                PARA("tensor")
                pe_s[(p, kc, h, head)] = OP(
                    "tensor", sgrp(p, kc, h, head, r0, r1, pbase), "pe")

            emit_sgrp(p, 0, 0, 0)
            emit_sgrp(p, 0, 0, 1)
            for kc in range(SC):
                emit_exps(p, kc, 0)
                emit_sgrp(p, kc, 1, 0)
                emit_sgrp(p, kc, 1, 1)
                emit_exps(p, kc, 1)
                if kc + 1 < SC:
                    emit_sgrp(p, kc + 1, 0, 0)
                    emit_sgrp(p, kc + 1, 0, 1)
                if kc > 0:
                    emit_zvp(p, kc - 1, ac_e[(p, kc, 0, 0)])
                    emit_ctx(p, kc - 1)
                if p == NPAIR - 1 and 2 <= kc < 14 and quant_q:
                    kind, qfn = quant_q.pop(0)
                    if kind == "xq":
                        W("vector", sems["pe"], pe_qk[NPAIR - 1])
                        W("vector", sems["ve"], ve_srep)
                    else:
                        W("vector", sems["w2"], 32)
                    quant_done[kind] = OP("vector", qfn, "ve")

            ac_flush = OP("scalar", lambda e, s: e.activation(
                flushb[:], srp[:, 0:32], AF.Exp,
                bias=expb[:], scale=exps[:]), "ac")
            emit_zvp(p, SC - 1, ac_flush)
            pending_ctx[0] = p

        emit_final_ctx(emit_ctx_holder[0])
        if True:
            pass

        W("gpsimd", sems["pe"], pe_qk[NPAIR - 1])
        OP("gpsimd", wload(wq_sb, wl_h), "w2", 16)
        w2_done = OP("gpsimd", lambda e, s: e.dma_start(
            wk_sb[:].rearrange("p (c h) -> p c h", c=DC),
            fk_h.rearrange("(c p) h -> p c h", p=128)), "w2", 16)

        ve_kq = quant_done.get("kq")
        ve_xq = max(v for v in quant_done.values())

        ve_out = [None] * DC
        st_par = [0, 0]
        for fc in range(DC):
            par = "x3a" if fc % 2 == 0 else "x3b"
            stp = "sta" if fc % 2 == 0 else "stb"
            if fc >= 2:
                W("sync", sems[stp], st_par[fc % 2])
            if fc <= 1:
                W("sync", sems["pe"], pe_rrep)
            x3 = OP("sync", (lambda fc: lambda e, s: e.dma_start(
                x32[fc % 2][:], x_h[fc * 128:(fc + 1) * 128, :]))(fc), par, 16)
            ps = PSA if fc % 2 == 0 else PSB
            if fc == 0:
                W("tensor", sems["ve"], ve_xq)
                W("tensor", sems["ve"], ve_bc)
                W("tensor", sems["ve"], ve_ctxev[NPAIR - 1])
            if fc >= 2:
                W("tensor", sems["ve"], ve_out[fc - 2])
            def ep(fc, ps):
                def f(e, s):
                    last = None
                    for sl in range(4):
                        e.matmul(ps[:, sl * 512:(sl + 1) * 512],
                                 bcomb_row[0:1, fc * 128:(fc + 1) * 128],
                                 ones512b[:], start=True, stop=False,
                                 skip_group_check=True)
                    for g in range(16):
                        if g < DC:
                            lhs = wq_sb[:, g * D + fc * 128: g * D + fc * 128 + 128]
                            rhs_t, rbase = ctx_sb, g * S
                        else:
                            c = g - DC
                            lhs = wk_sb[:, c * D + fc * 128: c * D + fc * 128 + 128]
                            rhs_t, rbase = xn, c * S
                        for sl in range(4):
                            last = e.matmul(ps[:, sl * 512:(sl + 1) * 512], lhs,
                                            rhs_t[:, rbase + sl * 512: rbase + (sl + 1) * 512],
                                            start=False, stop=(g == 15),
                                            skip_group_check=True)
                    return last
                return f
            pe_ep = OP("tensor", ep(fc, ps), "pe")
            W("vector", sems["pe"], pe_ep)
            W("vector", sems[par], x3)
            ve_out[fc] = OP("vector",
                            (lambda fc, ps: lambda e, s: e.tensor_add(
                                x32[fc % 2][:], ps[:, :], x32[fc % 2][:]))(fc, ps), "ve")
            W("sync", sems["ve"], ve_out[fc])
            st_par[fc % 2] = OP("sync", (lambda fc: lambda e, s: e.dma_start(
                y_h[fc * 128:(fc + 1) * 128, :], x32[fc % 2][:]))(fc), stp, 16)

        W("sync", sems["sta"], st_par[0])
        W("sync", sems["stb"], st_par[1])
        if dbg:
            dbg_list = [("d_xn", xn), ("d_v", vsb), ("d_ctx", ctx_sb),
                        ("d_qt", qt[1]), ("d_kt", kt[1]), ("d_wl", wq_sb),
                        ("d_kq", wk_sb), ("d_srep", srp), ("d_bcomb", bcomb_row),
                        ("d_acc0", acc[0]), ("d_acc1", acc[1]),
                        ("d_vp0", vp[0]), ("d_vp1", vp[1]),
                        ("d_qt0", qt[0]), ("d_kt0", kt[0]), ("d_vbr", vbr), ("d_uE", uE)]
            ndbg = 0
            for nm, t in dbg_list:
                sh = [t.shape[0] if hasattr(t, "shape") else 128, None]
                ap = t[:]
                shp = list(ap.shape)
                dty = ap.dtype
                dh = nc.dram_tensor(nm, shp, dty, kind="ExternalOutput").ap()
                OP("sync", (lambda dh, ap: lambda e, s: e.dma_start(dh, ap))(dh, ap),
                   "sta", 16)
                ndbg += 1
            W("sync", sems["sta"], em.counts["sta"])

        engines = {"sync": block.sync, "gpsimd": block.gpsimd,
                   "vector": block.vector, "scalar": block.scalar,
                   "tensor": block.tensor}

        def emit(name):
            def run(eng):
                for item in em.ops[name]:
                    if item[0] == "wait":
                        _, sem, val = item
                        eng.wait_ge(sem, val)
                    else:
                        _, fn, inc_sem, inc_n = item
                        inst = fn(eng, sems)
                        if inc_sem:
                            inst.then_inc(sems[inc_sem], inc_n)
            return run

        for name, deco in engines.items():
            deco(emit(name))

    return nc


_NC_CACHE = {}


def _get_nc():
    if "nc" not in _NC_CACHE:
        _NC_CACHE["nc"] = build_program()
    return _NC_CACHE["nc"]


def kernel(L0_attn_x, L0_attn_q_w, L0_attn_q_b, L0_attn_k_w, L0_attn_k_b,
           L0_attn_v_w, L0_attn_v_b, L0_attn_l_w, L0_attn_l_b,
           L0_ffn_kernel, L0_ffn_input_scale, L0_ffn_kernel_scale, L0_ffn_bias):
    x = np.asarray(L0_attn_x)
    cc = np.ascontiguousarray
    wq = cc(np.asarray(L0_attn_q_w).reshape(D, D).T)
    wk = cc(np.asarray(L0_attn_k_w).reshape(D, D).T)
    wv = cc(np.asarray(L0_attn_v_w).reshape(D, D).T)
    wl = cc(np.asarray(L0_attn_l_w).T)
    fk = cc(np.asarray(L0_ffn_kernel))
    qb = cc(np.asarray(L0_attn_q_b).reshape(D).reshape(NPAIR, 128).T)
    kb = cc(np.asarray(L0_attn_k_b).reshape(D).reshape(NPAIR, 128).T)
    vbr = cc(np.broadcast_to(np.asarray(L0_attn_v_b).reshape(1, D), (128, D)))
    lbr = cc(np.asarray(L0_attn_l_b).reshape(1, D))
    fbr = cc(np.asarray(L0_ffn_bias).reshape(1, D))
    s_in = np.asarray(L0_ffn_input_scale).reshape(1, 1)
    s_k = np.asarray(L0_ffn_kernel_scale).reshape(1, 1)
    sclr = cc(np.concatenate([np.broadcast_to(s_in, (128, 128)),
                              np.broadcast_to(s_k, (128, 128))], axis=1))

    in_maps = []
    for b in range(NC):
        in_maps.append({
            "x": cc(x[b].T), "wq": wq, "wk": wk, "wv": wv, "wl": wl, "fk": fk,
            "qb": qb, "kb": kb, "vbr": vbr, "lbr": lbr, "fbr": fbr, "sclr": sclr,
        })
    nc = _get_nc()
    res = run_bass_kernel_spmd(nc, in_maps, core_ids=list(range(NC)))
    out = np.stack([np.asarray(r["y"]).T for r in res.results])
    return np.ascontiguousarray(out.astype(np.float32))


# version 17
# speedup vs baseline: 1.0041x; 1.0041x over previous
import contextlib
import numpy as np
import concourse.bass as bass
import concourse.mybir as mybir
from concourse.bass_utils import run_bass_kernel_spmd

dt = mybir.dt
F32 = dt.float32
BF16 = dt.bfloat16
FP8 = dt.float8e4
AF = mybir.ActivationFunctionType
ALU = mybir.AluOpType

D = 1024
S = 2048
H = 16
P = 64
NC = 8
DC = D // 128
SC = S // 128
NPAIR = H // 2
EXP_SHIFT = -16.0


class Emitter:
    def __init__(self):
        self.ops = {e: [] for e in ("sync", "gpsimd", "vector", "scalar", "tensor")}
        self.counts = {}
        self.last_wait = {}

    def inc(self, sem, n=1):
        self.counts[sem] = self.counts.get(sem, 0) + n
        return self.counts[sem]

    def wait(self, engine, sem, val):
        if val is None or (isinstance(val, int) and val <= 0):
            return
        key = (engine, sem)
        if self.last_wait.get(key, -1) >= val:
            return
        self.last_wait[key] = val
        self.ops[engine].append(("wait", sem, val))

    def op(self, engine, fn, inc_sem=None, inc_n=1):
        val = self.inc(inc_sem, inc_n) if inc_sem else None
        self.ops[engine].append(("op", fn, inc_sem, inc_n))
        return val


def build_program(dbg=False, paranoid=False):
    nc = bass.Bass("TRN2", target_bir_lowering=False, debug=False)

    x_h = nc.dram_tensor("x", [D, S], F32, kind="ExternalInput").ap()
    wq_h = nc.dram_tensor("wq", [D, D], F32, kind="ExternalInput").ap()
    wk_h = nc.dram_tensor("wk", [D, D], F32, kind="ExternalInput").ap()
    wv_h = nc.dram_tensor("wv", [D, D], F32, kind="ExternalInput").ap()
    wl_h = nc.dram_tensor("wl", [D, D], F32, kind="ExternalInput").ap()
    fk_h = nc.dram_tensor("fk", [D, D], BF16, kind="ExternalInput").ap()
    qb_h = nc.dram_tensor("qb", [128, NPAIR], F32, kind="ExternalInput").ap()
    kb_h = nc.dram_tensor("kb", [128, NPAIR], F32, kind="ExternalInput").ap()
    vbr_h = nc.dram_tensor("vbr", [128, D], F32, kind="ExternalInput").ap()
    lbr_h = nc.dram_tensor("lbr", [1, D], F32, kind="ExternalInput").ap()
    fbr_h = nc.dram_tensor("fbr", [1, D], BF16, kind="ExternalInput").ap()
    sc_h = nc.dram_tensor("sclr", [128, 256], F32, kind="ExternalInput").ap()
    y_h = nc.dram_tensor("y", [D, S], F32, kind="ExternalOutput").ap()

    em = Emitter()
    W, OP = em.wait, em.op
    cm = contextlib.ExitStack()
    with cm:
        E = cm.enter_context
        xbf = [E(nc.sbuf_tensor(f"xbf{i}", [128, S], BF16)) for i in range(2)]
        xn = E(nc.sbuf_tensor("xn", [128, DC * S], BF16))
        vsb = E(nc.sbuf_tensor("vsb", [128, SC * D], BF16))
        ctx_sb = E(nc.sbuf_tensor("ctxs", [128, NPAIR * S], BF16))
        wq_sb = E(nc.sbuf_tensor("wqs", [128, DC * D], BF16))
        wk_sb = E(nc.sbuf_tensor("wks", [128, DC * D], BF16))
        uE = E(nc.sbuf_tensor("uE", [128, 4 * S], BF16))
        uE2 = E(nc.sbuf_tensor("uE2", [128, 2 * S], BF16))
        qt = [E(nc.sbuf_tensor(f"qt{i}", [128, S], BF16)) for i in range(2)]
        xsq = [qt[0], qt[0]]
        kt = [E(nc.sbuf_tensor(f"kt{i}", [128, S], BF16)) for i in range(2)]
        x32 = [E(nc.sbuf_tensor(f"x32{i}", [128, S], F32)) for i in range(2)]
        f8 = E(nc.sbuf_tensor("f8", [128, 1024], FP8))
        m_row = x32[1][0:1, :]
        v_row = x32[0][0:1, :]
        vbr = E(nc.sbuf_tensor("vbrs", [128, D], BF16))
        qb_sb = E(nc.sbuf_tensor("qbs", [128, NPAIR], F32))
        kb_sb = E(nc.sbuf_tensor("kbs", [128, NPAIR], F32))
        lbr_sb = E(nc.sbuf_tensor("lbrs", [1, D], F32))
        fbr_sb = E(nc.sbuf_tensor("fbrs", [1, D], BF16))
        bcomb_row = E(nc.sbuf_tensor("bcombr", [1, D], BF16))
        sc_sb = E(nc.sbuf_tensor("scs", [128, 256], F32))
        srp = E(nc.sbuf_tensor("srp", [128, 3 * 128], F32))
        ones512b = E(nc.sbuf_tensor("ones512b", [1, 512], BF16))
        ones_row = E(nc.sbuf_tensor("onesr", [1, 128], F32))
        c1024 = E(nc.sbuf_tensor("c1024", [128, 1], BF16))
        expb = E(nc.sbuf_tensor("expb", [128, 1], F32))
        exps = E(nc.sbuf_tensor("exps", [128, 1], F32))
        eps_col = E(nc.sbuf_tensor("epsc", [1, 1], F32))
        one_col = E(nc.sbuf_tensor("onec", [1, 1], F32))
        acc = [E(nc.sbuf_tensor(f"acc{i}", [128, 4], F32)) for i in range(2)]
        flushb = E(nc.sbuf_tensor("flushb", [128, 32], BF16))
        vp = [E(nc.sbuf_tensor(f"vp{i}", [128, 128], BF16)) for i in range(2)]

        PSA = E(nc.psum_tensor("psa", [128, S], F32))
        PSB = E(nc.psum_tensor("psb", [128, S], F32))

        sems = {n: E(nc.semaphore(n)) for n in
                ["xa", "xb", "xc", "xd", "w", "wb", "wc", "w2", "sm",
                 "x3a", "x3b", "sta", "stb", "pe", "ve", "ac"]}
        block = E(nc.Block())

        OP("vector", lambda e, s: e.memset(ones_row[:], 1.0))
        OP("vector", lambda e, s: e.memset(c1024[:], 1.0 / 1024.0))
        OP("vector", lambda e, s: e.memset(expb[:], EXP_SHIFT))
        OP("vector", lambda e, s: e.memset(eps_col[:], 1e-5))
        OP("vector", lambda e, s: e.memset(one_col[:], 1.0))
        OP("vector", lambda e, s: e.memset(ones512b[:], 1.0))
        OP("vector", lambda e, s: e.memset(x32[0][32:33, 0:128], 1.0))
        ve_consts = OP("vector", lambda e, s: e.memset(exps[:], 0.125), "ve")

        sm_small = None
        for (dst, src) in ((qb_sb, qb_h), (kb_sb, kb_h),
                           (lbr_sb, lbr_h), (fbr_sb, fbr_h), (sc_sb, sc_h)):
            sm_small = OP("sync",
                          (lambda d, s0: lambda e, s: e.dma_start(d[:], s0))(dst, src),
                          "sm", 16)

        def wload(dst_t, src_ap):
            def f(e, s):
                o = dst_t[:].rearrange("p (c h) -> p c h", c=DC)
                i = src_ap.rearrange("(c p) h -> p c h", p=128)
                return e.dma_start(o, i)
            return f

        W("vector", sems["sm"], sm_small)
        ve_bc = OP("vector", lambda e, s: e.tensor_add(bcomb_row[:], lbr_sb[:], fbr_sb[:]), "ve")
        OP("vector", lambda e, s: e.reciprocal(srp[:, 0:128], sc_sb[:, 0:128]))
        OP("vector", lambda e, s: e.reciprocal(srp[:, 128:256], sc_sb[:, 128:256]))
        ve_srep = OP("vector", lambda e, s: e.tensor_mul(
            srp[:, 256:384], sc_sb[:, 0:128], sc_sb[:, 128:256]), "ve")

        ve_xsq = [None] * DC
        pe_stats = [None] * DC
        for c in range(DC):
            par = ("xa", "xb", "xc", "xd")[c % 4]
            if c >= 4:
                W("gpsimd", sems["ve"], ve_xsq[c - 4])
                W("gpsimd", sems["pe"], pe_stats[c - 4])
            xd = OP("gpsimd",
                    (lambda c: lambda e, s: e.dma_start(
                        xn[:, c * S:(c + 1) * S],
                        x_h[c * 128:(c + 1) * 128, :]))(c), par, 16)
            W("vector", sems[par], xd)
            if c >= 1:
                W("vector", sems["pe"], pe_stats[c - 1])
            ve_xsq[c] = OP("vector",
                           (lambda c: lambda e, s: e.tensor_mul(
                               xsq[c % 2][:], xn[:, c * S:(c + 1) * S],
                               xn[:, c * S:(c + 1) * S]))(c), "ve")
            W("tensor", sems[par], xd)
            W("tensor", sems["ve"], ve_xsq[c])
            if c == 0:
                W("tensor", sems["ve"], ve_consts)
                W("tensor", sems["ve"], ve_srep)
            def stats(c):
                def f(e, s):
                    last = None
                    for sl in range(4):
                        last = e.matmul(PSA[0:1, sl * 512:(sl + 1) * 512], c1024[:],
                                        xn[:, c * S + sl * 512: c * S + (sl + 1) * 512],
                                        start=(c == 0), stop=(c == DC - 1),
                                        skip_group_check=True)
                        last = e.matmul(PSB[0:1, sl * 512:(sl + 1) * 512], c1024[:],
                                        xsq[c % 2][:, sl * 512:(sl + 1) * 512],
                                        start=(c == 0), stop=(c == DC - 1),
                                        skip_group_check=True)
                    return last
                return f
            pe_stats[c] = OP("tensor", stats(c), "pe")

        OP("gpsimd", wload(uE, wv_h), "w", 16)
        OP("gpsimd", lambda e, s: e.dma_start(vbr[:], vbr_h), "wb", 16)
        OP("gpsimd", wload(wq_sb, wq_h), "wc", 16)
        w_done = OP("gpsimd", wload(wk_sb, wk_h), "wc", 16)

        W("vector", sems["pe"], pe_stats[DC - 1])
        OP("vector", lambda e, s: e.tensor_copy(m_row, PSA[0:1, :]))
        OP("vector", lambda e, s: e.tensor_mul(v_row, m_row, m_row))
        ve_var = OP("vector", lambda e, s: e.tensor_sub(v_row, PSB[0:1, :], v_row), "ve")

        def repl(ps, row, ones):
            def f(e, s):
                last = None
                for sl in range(4):
                    last = e.matmul(ps[:, sl * 512:(sl + 1) * 512], ones,
                                    row[:, sl * 512:(sl + 1) * 512], start=True, stop=True)
                return last
            return f
        std_row = x32[1][64:65, :]
        rstd_row = x32[1][32:33, :]
        ones32 = x32[0][32:33, 0:128]
        W("tensor", sems["ve"], ve_var)
        pe_mrep = OP("tensor", repl(PSA, m_row, ones_row[:]), "pe")
        W("scalar", sems["ve"], ve_var)
        ac_sqrt = OP("scalar", lambda e, s: e.activation(
            std_row, v_row, AF.Sqrt, bias=eps_col[:], scale=one_col[:]), "ac")
        W("vector", sems["ac"], ac_sqrt)
        ve_rstd = OP("vector", lambda e, s: e.reciprocal(rstd_row, std_row), "ve")
        W("tensor", sems["ve"], ve_rstd)
        pe_rrep = OP("tensor", repl(PSB, rstd_row, ones32), "pe")

        ve_xn = [None] * DC
        W("vector", sems["pe"], pe_rrep)
        for c in range(DC):
            def lnapply(c):
                def f(e, s):
                    e.tensor_sub(xsq[c % 2][:], xn[:, c * S:(c + 1) * S], PSA[:, :])
                    return e.tensor_mul(xn[:, c * S:(c + 1) * S], xsq[c % 2][:], PSB[:, :])
                return f
            ve_xn[c] = OP("vector", lnapply(c), "ve")

        W("tensor", sems["ve"], ve_xn[DC - 1])
        W("tensor", sems["w"], 16)
        ve_vev = [None] * SC
        pe_v = [None] * SC
        W("vector", sems["wb"], 16)
        for j in range(SC):
            half = (j % 2) * 1024
            if j >= 2:
                W("tensor", sems["ve"], ve_vev[j - 2])
            def vproj(j, half):
                def f(e, s):
                    last = None
                    for c in range(DC):
                        for sl in range(2):
                            last = e.matmul(
                                PSA[:, half + sl * 512: half + (sl + 1) * 512],
                                xn[:, c * S + j * 128: c * S + (j + 1) * 128],
                                uE[:, c * D + sl * 512: c * D + (sl + 1) * 512],
                                start=(c == 0), stop=(c == DC - 1),
                                skip_group_check=True)
                    return last
                return f
            pe_v[j] = OP("tensor", vproj(j, half), "pe")
            W("vector", sems["pe"], pe_v[j])
            ve_vev[j] = OP("vector",
                           (lambda j, half: lambda e, s: e.tensor_add(
                               vsb[:, j * D:(j + 1) * D], PSA[:, half:half + 1024],
                               vbr[:]))(j, half), "ve")

        def PARA(engine):
            if not paranoid:
                return
            for snm in ("pe", "ve", "ac"):
                v = em.counts.get(snm, 0)
                if v > 0:
                    W(engine, sems[snm], v)
        pe_qk = [None] * NPAIR
        ve_qk_ready = [None] * NPAIR
        quant_q = []
        quant_done = {}

        def kq_op(sl):
            def f(e, s):
                last = None
                for hf in range(2):
                    for st_ in range(8):
                        a = sl * S + hf * 1024 + st_ * 128
                        e.tensor_mul(f8[:, st_ * 128:(st_ + 1) * 128],
                                     wk_sb[:, a:a + 128], srp[:, 128:256])
                    for st_ in range(8):
                        a = sl * S + hf * 1024 + st_ * 128
                        last = e.tensor_mul(wk_sb[:, a:a + 128],
                                            f8[:, st_ * 128:(st_ + 1) * 128],
                                            srp[:, 256:384])
                return last
            return f

        def xq_op(c):
            def f(e, s):
                last = None
                for hf in range(2):
                    for st_ in range(8):
                        a = c * S + hf * 1024 + st_ * 128
                        e.tensor_mul(f8[:, st_ * 128:(st_ + 1) * 128],
                                     xn[:, a:a + 128], srp[:, 0:128])
                    last = e.tensor_copy(xn[:, c * S + hf * 1024: c * S + hf * 1024 + 1024],
                                         f8[:])
                return last
            return f

        for _c in range(DC):
            quant_q.append(("xq", xq_op(_c)))
        for _sl in range(4):
            quant_q.append(("kq", kq_op(_sl)))
        ve_ctxev = [None] * NPAIR
        pe_s = {}
        ac_e = {}
        ve_vp = {}
        pe_ctx = {}
        pending_ctx = [None]
        emit_ctx_holder = [None]

        def emit_final_ctx(em_ctx_fn):
            pp = pending_ctx[0]
            if pp is None:
                return
            pending_ctx[0] = None
            em_ctx_fn(pp, SC - 1)
            W("vector", sems["pe"], pe_ctx[(pp, SC - 1)])
            ve_ctxev[pp] = OP("vector",
                              (lambda pp: lambda e, s: e.tensor_copy(
                                  ctx_sb[:, pp * S:(pp + 1) * S], PSB[:, :]))(pp), "ve")

        for p in range(NPAIR):
            if p == 0:
                W("tensor", sems["ve"], ve_vev[SC - 1])
                W("tensor", sems["wc"], 16 * 2)
            else:
                W("tensor", sems["ac"], ac_e[(p - 1, SC - 1, 1, 1)])

            def qk_slice(p, wsb, ps, pslo, sl):
                def f(e, s):
                    last = None
                    for c in range(DC):
                        last = e.matmul(
                            ps[:, pslo + sl * 512: pslo + (sl + 1) * 512],
                            wsb[:, c * D + p * 128: c * D + p * 128 + 128],
                            xn[:, c * S + sl * 512: c * S + (sl + 1) * 512],
                            start=(c == 0), stop=(c == DC - 1),
                            skip_group_check=True)
                    return last
                return f

            def emit_qk(p, wsb, ps, dst, bias):
                last_ve = None
                for sl in range(4):
                    pe_sl = OP("tensor", qk_slice(p, wsb, ps, 0, sl), "pe")
                    W("vector", sems["pe"], pe_sl)
                    def ev(dst, ps, bias, sl, p):
                        def f(e, s):
                            return e.tensor_scalar(
                                dst[p % 2][:, sl * 512:(sl + 1) * 512],
                                ps[:, sl * 512:(sl + 1) * 512],
                                bias[:, p:p + 1], None, ALU.add)
                        return f
                    last_ve = OP("vector", ev(dst, ps, bias, sl, p), "ve")
                return last_ve

            emit_qk(p, wq_sb, PSA, qt, qb_sb)
            if pending_ctx[0] is not None:
                emit_final_ctx(emit_ctx_holder[0])
                W("tensor", sems["ve"], ve_ctxev[p - 1])
            ve_qk_ready[p] = emit_qk(p, wk_sb, PSB, kt, kb_sb)
            pe_qk[p] = em.counts["pe"]

            def eslc(kc, head, lo, ln):
                par = kc % 4
                if par < 2:
                    return uE[:, par * 2 * S + head * S + lo: par * 2 * S + head * S + lo + ln]
                if par == 2:
                    return uE2[:, head * S + lo: head * S + lo + ln]
                return xbf[head][:, lo: lo + ln]

            def emit_exps(p, kc, h):
                W("scalar", sems["ve"], ve_consts)
                if h == 0:
                    if (p, kc - 2) in ve_vp:
                        W("scalar", sems["ve"], ve_vp[(p, kc - 2)])
                    elif p >= 1 and kc < 2:
                        W("scalar", sems["ve"], ve_vp[(p - 1, SC - 2 + kc)])
                    if (p, kc - 4) in pe_ctx:
                        W("scalar", sems["pe"], pe_ctx[(p, kc - 4)])
                    elif p >= 1 and kc < 4:
                        W("scalar", sems["pe"], pe_ctx[(p - 1, SC - 4 + kc)])
                for head in range(2):
                    W("scalar", sems["pe"], pe_s[(p, kc, h, head)])
                    apar = kc % 2
                    def egrp(kc, head, h, apar):
                        def f(e, s):
                            return e.activation(
                                eslc(kc, head, h * 1024, 1024),
                                PSA[:, head * 1024: head * 1024 + 1024],
                                AF.Exp, bias=expb[:], scale=exps[:],
                                accum_out=acc[apar][:, h * 2 + head: h * 2 + head + 1])
                        return f
                    PARA("scalar")
                    ac_e[(p, kc, h, head)] = OP("scalar", egrp(kc, head, h, apar), "ac")

            def emit_zvp(p, j, ac_gate):
                W("vector", sems["ac"], ac_gate)
                if (p, j - 2) in pe_ctx:
                    W("vector", sems["pe"], pe_ctx[(p, j - 2)])
                elif p >= 1 and j < 2:
                    W("vector", sems["pe"], pe_ctx[(p - 1, SC - 2 + j)])
                def zvp(p, j):
                    par = j % 2
                    def f(e, s):
                        e.tensor_add(sc_sb[:, 0:64],
                                     acc[par][:, 0:1].broadcast_to([128, 64]),
                                     acc[par][:, 2:3].broadcast_to([128, 64]))
                        e.tensor_add(sc_sb[:, 64:128],
                                     acc[par][:, 1:2].broadcast_to([128, 64]),
                                     acc[par][:, 3:4].broadcast_to([128, 64]))
                        e.reciprocal(sc_sb[:, 128:256], sc_sb[:, 0:128])
                        return e.tensor_mul(
                            vp[par][:], vsb[:, j * D + p * 128: j * D + (p + 1) * 128],
                            sc_sb[:, 128:256])
                    return f
                PARA("vector")
                ve_vp[(p, j)] = OP("vector", zvp(p, j), "ve")

            def emit_ctx(p, kc1):
                W("tensor", sems["ve"], ve_vp[(p, kc1)])
                def cgrp(p, kc1):
                    vpar = kc1 % 2
                    def f(e, s):
                        last = None
                        for sl in range(4):
                            last = e.matmul(
                                PSB[0:64, sl * 512:(sl + 1) * 512],
                                vp[vpar][:, 0:64],
                                eslc(kc1, 0, sl * 512, 512),
                                start=(kc1 == 0), stop=(kc1 == SC - 1),
                                tile_position=(0, 0), skip_group_check=True)
                            last = e.matmul(
                                PSB[64:128, sl * 512:(sl + 1) * 512],
                                vp[vpar][:, 64:128],
                                eslc(kc1, 1, sl * 512, 512),
                                start=(kc1 == 0), stop=(kc1 == SC - 1),
                                tile_position=(0, 64), skip_group_check=True)
                        return last
                    return f
                PARA("tensor")
                pe_ctx[(p, kc1)] = OP("tensor", cgrp(p, kc1), "pe")
            emit_ctx_holder[0] = emit_ctx

            def emit_sgrp(p, kc, h, head):
                if kc == 0 and h == 0:
                    if head == 0:
                        W("tensor", sems["ve"], ve_qk_ready[p])
                else:
                    prev = (p, kc, 0, head) if h == 1 else (p, kc - 1, 1, head)
                    W("tensor", sems["ac"], ac_e[prev])
                r0, r1 = (0, 64) if head == 0 else (64, 128)
                pbase = head * 1024
                def sgrp(p, kc, h, head, r0, r1, pbase):
                    def f(e, s):
                        last = None
                        for sl in range(2):
                            q0 = h * 1024 + sl * 512
                            last = e.matmul(
                                PSA[:, pbase + sl * 512: pbase + (sl + 1) * 512],
                                kt[p % 2][r0:r1, kc * 128:(kc + 1) * 128],
                                qt[p % 2][r0:r1, q0:q0 + 512],
                                start=True, stop=True)
                        return last
                    return f
                PARA("tensor")
                pe_s[(p, kc, h, head)] = OP(
                    "tensor", sgrp(p, kc, h, head, r0, r1, pbase), "pe")

            emit_sgrp(p, 0, 0, 0)
            emit_sgrp(p, 0, 0, 1)
            for kc in range(SC):
                emit_exps(p, kc, 0)
                emit_sgrp(p, kc, 1, 0)
                emit_sgrp(p, kc, 1, 1)
                emit_exps(p, kc, 1)
                if kc + 1 < SC:
                    emit_sgrp(p, kc + 1, 0, 0)
                    emit_sgrp(p, kc + 1, 0, 1)
                if kc > 0:
                    emit_zvp(p, kc - 1, ac_e[(p, kc, 0, 0)])
                    emit_ctx(p, kc - 1)
                if p == NPAIR - 1 and 2 <= kc < 14 and quant_q:
                    kind, qfn = quant_q.pop(0)
                    if kind == "xq":
                        W("vector", sems["pe"], pe_qk[NPAIR - 1])
                        W("vector", sems["ve"], ve_srep)
                    else:
                        W("vector", sems["w2"], 32)
                    quant_done[kind] = OP("vector", qfn, "ve")

            ac_flush = OP("scalar", lambda e, s: e.activation(
                flushb[:], srp[:, 0:32], AF.Exp,
                bias=expb[:], scale=exps[:]), "ac")
            emit_zvp(p, SC - 1, ac_flush)
            pending_ctx[0] = p

        emit_final_ctx(emit_ctx_holder[0])
        if True:
            pass

        W("gpsimd", sems["pe"], pe_qk[NPAIR - 1])
        OP("gpsimd", wload(wq_sb, wl_h), "w2", 16)
        w2_done = OP("gpsimd", lambda e, s: e.dma_start(
            wk_sb[:].rearrange("p (c h) -> p c h", c=DC),
            fk_h.rearrange("(c p) h -> p c h", p=128)), "w2", 16)

        ve_kq = quant_done.get("kq")
        ve_xq = max(v for v in quant_done.values())

        ve_out = [None] * DC
        st_par = [0, 0]
        for fc in range(DC):
            par = "x3a" if fc % 2 == 0 else "x3b"
            stp = "sta" if fc % 2 == 0 else "stb"
            if fc >= 2:
                W("scalar", sems[stp], st_par[fc % 2])
            if fc <= 1:
                W("scalar", sems["pe"], pe_rrep)
            x3 = OP("scalar", (lambda fc: lambda e, s: e.dma_start(
                x32[fc % 2][:], x_h[fc * 128:(fc + 1) * 128, :]))(fc), par, 16)
            ps = PSA if fc % 2 == 0 else PSB
            if fc == 0:
                W("tensor", sems["ve"], ve_xq)
                W("tensor", sems["ve"], ve_bc)
                W("tensor", sems["ve"], ve_ctxev[NPAIR - 1])
            if fc >= 2:
                W("tensor", sems["ve"], ve_out[fc - 2])
            def ep(fc, ps):
                def f(e, s):
                    last = None
                    for sl in range(4):
                        e.matmul(ps[:, sl * 512:(sl + 1) * 512],
                                 bcomb_row[0:1, fc * 128:(fc + 1) * 128],
                                 ones512b[:], start=True, stop=False,
                                 skip_group_check=True)
                    for g in range(16):
                        if g < DC:
                            lhs = wq_sb[:, g * D + fc * 128: g * D + fc * 128 + 128]
                            rhs_t, rbase = ctx_sb, g * S
                        else:
                            c = g - DC
                            lhs = wk_sb[:, c * D + fc * 128: c * D + fc * 128 + 128]
                            rhs_t, rbase = xn, c * S
                        for sl in range(4):
                            last = e.matmul(ps[:, sl * 512:(sl + 1) * 512], lhs,
                                            rhs_t[:, rbase + sl * 512: rbase + (sl + 1) * 512],
                                            start=False, stop=(g == 15),
                                            skip_group_check=True)
                    return last
                return f
            pe_ep = OP("tensor", ep(fc, ps), "pe")
            W("vector", sems["pe"], pe_ep)
            W("vector", sems[par], x3)
            ve_out[fc] = OP("vector",
                            (lambda fc, ps: lambda e, s: e.tensor_add(
                                x32[fc % 2][:], ps[:, :], x32[fc % 2][:]))(fc, ps), "ve")
            W("sync", sems["ve"], ve_out[fc])
            if fc >= 2:
                W("sync", sems[stp], st_par[fc % 2])
            st_par[fc % 2] = OP("sync", (lambda fc: lambda e, s: e.dma_start(
                y_h[fc * 128:(fc + 1) * 128, :], x32[fc % 2][:]))(fc), stp, 16)

        W("sync", sems["sta"], st_par[0])
        W("sync", sems["stb"], st_par[1])
        if dbg:
            dbg_list = [("d_xn", xn), ("d_v", vsb), ("d_ctx", ctx_sb),
                        ("d_qt", qt[1]), ("d_kt", kt[1]), ("d_wl", wq_sb),
                        ("d_kq", wk_sb), ("d_srep", srp), ("d_bcomb", bcomb_row),
                        ("d_acc0", acc[0]), ("d_acc1", acc[1]),
                        ("d_vp0", vp[0]), ("d_vp1", vp[1]),
                        ("d_qt0", qt[0]), ("d_kt0", kt[0]), ("d_vbr", vbr), ("d_uE", uE)]
            ndbg = 0
            for nm, t in dbg_list:
                sh = [t.shape[0] if hasattr(t, "shape") else 128, None]
                ap = t[:]
                shp = list(ap.shape)
                dty = ap.dtype
                dh = nc.dram_tensor(nm, shp, dty, kind="ExternalOutput").ap()
                OP("sync", (lambda dh, ap: lambda e, s: e.dma_start(dh, ap))(dh, ap),
                   "sta", 16)
                ndbg += 1
            W("sync", sems["sta"], em.counts["sta"])

        engines = {"sync": block.sync, "gpsimd": block.gpsimd,
                   "vector": block.vector, "scalar": block.scalar,
                   "tensor": block.tensor}

        def emit(name):
            def run(eng):
                for item in em.ops[name]:
                    if item[0] == "wait":
                        _, sem, val = item
                        eng.wait_ge(sem, val)
                    else:
                        _, fn, inc_sem, inc_n = item
                        inst = fn(eng, sems)
                        if inc_sem:
                            inst.then_inc(sems[inc_sem], inc_n)
            return run

        for name, deco in engines.items():
            deco(emit(name))

    return nc


_NC_CACHE = {}


def _get_nc():
    if "nc" not in _NC_CACHE:
        _NC_CACHE["nc"] = build_program()
    return _NC_CACHE["nc"]


def kernel(L0_attn_x, L0_attn_q_w, L0_attn_q_b, L0_attn_k_w, L0_attn_k_b,
           L0_attn_v_w, L0_attn_v_b, L0_attn_l_w, L0_attn_l_b,
           L0_ffn_kernel, L0_ffn_input_scale, L0_ffn_kernel_scale, L0_ffn_bias):
    x = np.asarray(L0_attn_x)
    cc = np.ascontiguousarray
    wq = cc(np.asarray(L0_attn_q_w).reshape(D, D).T)
    wk = cc(np.asarray(L0_attn_k_w).reshape(D, D).T)
    wv = cc(np.asarray(L0_attn_v_w).reshape(D, D).T)
    wl = cc(np.asarray(L0_attn_l_w).T)
    fk = cc(np.asarray(L0_ffn_kernel))
    qb = cc(np.asarray(L0_attn_q_b).reshape(D).reshape(NPAIR, 128).T)
    kb = cc(np.asarray(L0_attn_k_b).reshape(D).reshape(NPAIR, 128).T)
    vbr = cc(np.broadcast_to(np.asarray(L0_attn_v_b).reshape(1, D), (128, D)))
    lbr = cc(np.asarray(L0_attn_l_b).reshape(1, D))
    fbr = cc(np.asarray(L0_ffn_bias).reshape(1, D))
    s_in = np.asarray(L0_ffn_input_scale).reshape(1, 1)
    s_k = np.asarray(L0_ffn_kernel_scale).reshape(1, 1)
    sclr = cc(np.concatenate([np.broadcast_to(s_in, (128, 128)),
                              np.broadcast_to(s_k, (128, 128))], axis=1))

    in_maps = []
    for b in range(NC):
        in_maps.append({
            "x": cc(x[b].T), "wq": wq, "wk": wk, "wv": wv, "wl": wl, "fk": fk,
            "qb": qb, "kb": kb, "vbr": vbr, "lbr": lbr, "fbr": fbr, "sclr": sclr,
        })
    nc = _get_nc()
    res = run_bass_kernel_spmd(nc, in_maps, core_ids=list(range(NC)))
    out = np.stack([np.asarray(r["y"]).T for r in res.results])
    return np.ascontiguousarray(out.astype(np.float32))
